# version 22 (final)
# baseline (speedup 1.0000x reference)
import sys
import numpy as np
import ml_dtypes
from contextlib import ExitStack
import concourse.bass as bass
import concourse.mybir as mybir
from concourse.bass_utils import run_bass_kernel_spmd

F32 = mybir.dt.float32
BF16 = mybir.dt.bfloat16
AF = mybir.ActivationFunctionType
ALU = mybir.AluOpType
AX = mybir.AxisListType
BF = ml_dtypes.bfloat16

D = 2048
T = 1024
TT = 8
DC = 16
S = 2048
MAGIC = float(3 * 2 ** 22)
W0 = [1152 + 128 * j for j in range(TT)]
RMS_EPS = 1.1920928955078125e-07
ATTN_DBG = None
SC0 = float(1.0 / np.sqrt(np.float32(2048)))
SC1 = float(1.0 / np.sqrt(np.float32(128)))

ENGINES = ("sync", "gpsimd", "vector", "tensor", "scalar")
SEM_OF = {"sync": "dL", "gpsimd": "gp", "vector": "ve", "tensor": "pe", "scalar": "ac"}
DMA_SEM = {"sync": "dL", "gpsimd": "dW", "scalar": "dS"}


class Ev:
    __slots__ = ("sem", "val", "eng", "hard")

    def __init__(self, sem, val, eng, hard=False):
        self.sem, self.val, self.eng, self.hard = sem, val, eng, hard

    def h(self):
        return Ev(self.sem, self.val, self.eng, True)


class Prog:
    def __init__(self):
        self.nc = bass.Bass("TRN2", target_bir_lowering=False, debug=False)
        self.stack = ExitStack()
        self.semh = {}
        self.cnt = {}
        for nm in ("dL0", "dL1", "dL2", "dL3", "dW0", "dW1", "dW2", "dW3",
                   "dS0", "dS1", "dS2", "dS3", "dS4", "dS5", "dSA",
                   "ve", "pe", "ac", "gp", "cc"):
            self.semh[nm] = self.stack.enter_context(self.nc.semaphore(nm))
            self.cnt[nm] = 0
        self.block = self.stack.enter_context(self.nc.Block())
        self.hwm = {e: {} for e in ENGINES}
        self.labels = {e: [] for e in ENGINES}

    def sb(self, name, shape, dt):
        return self.stack.enter_context(self.nc.sbuf_tensor(name, shape, dt))

    def ps(self, name, shape, dt=F32):
        return self.stack.enter_context(self.nc.psum_tensor(name, shape, dt))

    def close(self):
        self.stack.close()


class Phase:

    def __init__(self, p: Prog):
        self.p = p
        self.ops = {e: [] for e in ENGINES}

    def do(self, eng, fn, after=(), mark=False, dma=False, cc=False, n=1):
        p = self.p
        ev = None
        sem = step = None
        if mark:
            if cc:
                sem, step = "cc", 1
            elif dma:
                sem, step = dma, 16
            else:
                sem, step = SEM_OF[eng], 1
            p.cnt[sem] += step * n
            ev = Ev(sem, p.cnt[sem], eng)
        self.ops[eng].append((fn, tuple(a for a in after if a is not None), sem, step, n,
                              sys._getframe(1).f_code.co_name))
        return ev

    def flush(self):
        p = self.p
        for eng in ENGINES:
            if not self.ops[eng]:
                continue
            ops = self.ops[eng]

            def body(h, ops=ops, eng=eng):
                hw = p.hwm[eng]
                for fn, after, sem, step, n, label in ops:
                    need = {}
                    for a in after:
                        if (a.eng == eng and not a.hard and a.sem in SEM_OF.values()):
                            continue
                        if a.val > need.get(a.sem, -1):
                            need[a.sem] = a.val
                    for s in sorted(need):
                        v = need[s]
                        if v > hw.get(s, -1):
                            h.wait_ge(p.semh[s], v)
                            hw[s] = v
                    with p.nc.named_scope(label):
                        out = fn(h)
                    if sem is not None:
                        lst = out if isinstance(out, (list, tuple)) else [out]
                        assert len(lst) == n, (eng, len(lst), n)
                        for ins in lst:
                            ins.then_inc(p.semh[sem], step)
            getattr(p.block, eng)(body)
        self.ops = {e: [] for e in ENGINES}


def rearr_cols(dram_ap, c0, c1):
    return dram_ap[:, c0:c1].rearrange("(kc p) m -> p kc m", p=128)


class RES:
    pass


def alloc_resources(p: Prog):
    r = RES()
    r.A1 = p.sb("A1", [128, 32768], BF16)
    r.A2 = p.sb("A2", [128, 32768], BF16)
    r.XN = p.sb("XNa", [128, 16384], BF16)
    r.QF = p.sb("QFa", [128, 16384], BF16)
    r.IDENT = p.sb("IDENT", [128, 128], BF16)
    r.CONSTS = p.sb("CONSTS", [128, 8], F32)
    r.STATS = p.sb("STATS", [128, 16], F32)
    r.B12 = p.sb("B12", [128, 64], F32)
    r.B3 = p.sb("B3", [1, 2048], F32)
    r.ONES = p.sb("ONESr", [1, 128], F32)
    r.SEL = p.sb("SEL", [128, 2], F32)
    r.SELT = p.sb("SELT", [2, 128], F32)
    r.SELB = p.sb("SELB", [128, 2], BF16)
    r.SSQ = p.sb("SSQ", [128, 144], F32)
    r.RST = p.sb("RST", [128, 8], F32)
    r.PSA = p.ps("PSA", [128, 2048], F32)
    r.PSB = p.ps("PSB", [128, 1024], F32)
    r.PSC = p.ps("PSC", [128, 512], F32)
    r.PST = p.ps("PSTp", [128, 1024], BF16)
    return r


def a_f32(arena, e0, n):
    return arena[:, e0:e0 + 2 * n].bitcast(F32)


def xn_chunk(r, c):
    return r.XN[:, c * 1024:(c + 1) * 1024]


def qf_chunk(r, c):
    return r.QF[:, c * 1024:(c + 1) * 1024]



def emit_consts(p, r, io):
    ph = Phase(p)

    def g_0(g):
        g.memset(r.CONSTS[:, 0:1], 1e-5)
        g.memset(r.CONSTS[:, 1:2], RMS_EPS)
        g.memset(r.CONSTS[:, 2:3], MAGIC)
        g.memset(r.CONSTS[:, 3:4], -MAGIC)
        g.memset(r.ONES[:, :], 1.0)
        return g.memset(r.IDENT[:, :], 0.0)
    ev_c0 = ph.do("gpsimd", g_0, mark=True)
    ev_c = ph.do("gpsimd", lambda g: g.affine_select(
        out=r.IDENT[:, :], in_=r.IDENT[:, :],
        compare_op=ALU.not_equal, fill=1.0,
        base=0, pattern=[[-1, 128]], channel_multiplier=1),
        after=[ev_c0.h()], mark=True)

    keys = [k for k in ("b12", "b3", "selc", "selr") if k in io]
    tgt = {"b12": r.B12, "b3": r.B3, "selc": r.SEL, "selr": r.SELT}

    def s_(s):
        return [s.dma_start(out=tgt[k][:, :], in_=io[k]) for k in keys]
    ev_s = ph.do("sync", s_, mark=True, dma="dL0", n=len(keys))
    if "selc" in io:
        ev_c = ph.do("gpsimd", lambda g: g.dma_start(
            out=r.SELB[:, :], in_=io["selc"]), after=[ev_c], mark=True, dma="dW0")
    ph.flush()
    return ev_c, ev_s


def emit_ln(p, r, x_dram, after0, sq_scratch=None):
    ph = Phase(p)
    ST = r.STATS
    scr = sq_scratch if sq_scratch is not None else r.PSA[:, :]
    xst = [a_f32(r.A1, 0, 2048), a_f32(r.A1, 4096, 2048)]
    xnst = [r.A1[:, 8192:10240], r.A1[:, 14336:16384]]
    ev_ld = [None] * TT
    ev_sq = [None] * TT
    ev_var = [None] * TT
    ev_sd = [None] * TT
    ev_norm = [None] * TT
    ev_tp = [[None, None] for _ in range(TT)]
    ev_ev = [[None, None] for _ in range(TT)]

    for t in range(TT):
        c0 = (t % 2) * 8
        after = list(after0)
        if t >= 2:
            after += [ev_norm[t - 2], ev_sq[t - 2]]
        ev_ld[t] = ph.do("sync", lambda s, t=t: s.dma_start(
            out=xst[t % 2][:, :], in_=x_dram[t * 128:(t + 1) * 128, :]),
            after=after, mark=True, dma="dL" + str(t % 2))
        after_sq = [ev_ld[t]] + list(after0)
        if t >= 2:
            after_sq.append(ev_var[t - 2])
        ev_sq[t] = ph.do("scalar", lambda a, t=t, c0=c0: a.activation(
            scr, xst[t % 2][:, :], AF.Square, accum_out=ST[:, c0 + 1:c0 + 2]),
            after=after_sq, mark=True)
        after_rd = [ev_ld[t]]
        if t >= 2:
            after_rd.append(ev_norm[t - 2])
        e_red = ph.do("vector", lambda v, t=t, c0=c0: v.reduce_sum(
            ST[:, c0:c0 + 1], xst[t % 2][:, :], axis=AX.X),
            after=after_rd, mark=True)
        e_m = ph.do("vector", lambda v, c0=c0: v.tensor_scalar(
            ST[:, c0 + 2:c0 + 3], ST[:, c0:c0 + 1], 1.0 / D, None, ALU.mult),
            after=[e_red.h()], mark=True)
        e_m2 = ph.do("vector", lambda v, c0=c0: v.tensor_mul(
            ST[:, c0 + 4:c0 + 5], ST[:, c0 + 2:c0 + 3], ST[:, c0 + 2:c0 + 3]),
            after=[e_m.h()], mark=True)
        ev_var[t] = ph.do("vector", lambda v, c0=c0: v.scalar_tensor_tensor(
            ST[:, c0 + 5:c0 + 6], ST[:, c0 + 1:c0 + 2], 1.0 / D, ST[:, c0 + 4:c0 + 5],
            ALU.mult, ALU.subtract),
            after=[ev_sq[t], e_m2.h()], mark=True)
        ev_sd[t] = ph.do("scalar", lambda a, c0=c0: a.activation(
            ST[:, c0 + 6:c0 + 7], ST[:, c0 + 5:c0 + 6], AF.Sqrt, bias=r.CONSTS[:, 0:1]),
            after=[ev_var[t]], mark=True)
        e_rc = ph.do("vector", lambda v, c0=c0: v.reciprocal(
            ST[:, c0 + 7:c0 + 8], ST[:, c0 + 6:c0 + 7]),
            after=[ev_sd[t]], mark=True)
        after_n = [ev_sd[t], e_rc.h()]
        if t >= 2:
            after_n.append(ev_tp[t - 2][1])
        ev_norm[t] = ph.do("vector", lambda v, t=t, c0=c0: v.tensor_scalar(
            xnst[t % 2][:, :], xst[t % 2][:, :], ST[:, c0 + 2:c0 + 3], ST[:, c0 + 7:c0 + 8],
            ALU.subtract, ALU.mult),
            after=after_n, mark=True)
        for rd in range(2):
            after_t = [ev_norm[t]]
            if rd == 1:
                after_t.append(ev_ev[t][0])
            elif t >= 1:
                after_t.append(ev_ev[t - 1][1])

            def tp(te, t=t, rd=rd):
                last = None
                for i8 in range(8):
                    c = rd * 8 + i8
                    last = te.transpose(r.PST[:, i8 * 128:(i8 + 1) * 128],
                                        xnst[t % 2][:, c * 128:(c + 1) * 128], r.IDENT[:, :])
                return last
            ev_tp[t][rd] = ph.do("tensor", tp, after=after_t, mark=True)

            def evc(a, t=t, rd=rd):
                dst = r.XN[:, :].rearrange("p (c n) -> p c n", c=DC)[
                    :, rd * 8:(rd + 1) * 8, t * 128:(t + 1) * 128]
                src = r.PST[:, :].rearrange("p (c n) -> p c n", c=8)
                return a.copy(dst, src)
            ev_ev[t][rd] = ph.do("scalar", evc, after=[ev_tp[t][rd]], mark=True)
    ph.flush()
    return ev_ev[TT - 1][1]


def emit_rope_proj(p, r, io, wname, ctab, stab, scaled, dst_kind, dram_k, xn_ev, after0):
    ph = Phase(p)
    wk_slot = [0, 4096]
    cs_off = [8192, 12288]
    tmpE = a_f32(r.A2, 16384, 1024)
    tmpO = a_f32(r.A2, 18432, 1024)
    tmp1 = a_f32(r.A2, 28672, 1024)
    tmp2 = a_f32(r.A2, 30720, 1024)
    kst = [r.A2[:, 20480:22528], r.A2[:, 22528:24576]]
    ev_w = [None] * 8
    ev_cs = [None] * 8
    ev_mm = [None] * 8
    ev_cp = [None] * 8
    ev_rope = [None] * 8
    ev_st = [None] * 8
    for pp in range(8):
        sl = pp % 2
        after_w = list(after0) if pp < 2 else [ev_mm[pp - 2]]

        def ldw(g, pp=pp, sl=sl):
            return [g.dma_start(
                out=r.A2[:, wk_slot[sl]:wk_slot[sl] + 2048],
                in_=io[wname][pp * 128:(pp + 1) * 128, :]),
                g.dma_start(
                out=r.A2[:, wk_slot[sl] + 2048:wk_slot[sl] + 4096],
                in_=io[wname][(pp + 8) * 128:(pp + 9) * 128, :])]
        ev_w[pp] = ph.do("gpsimd", ldw, after=after_w, mark=True, dma="dW" + str(pp % 2), n=2)
        after_c = list(after0) if pp < 2 else [ev_rope[pp - 2]]

        def ldc(s, pp=pp, sl=sl):
            return [s.dma_start(out=a_f32(r.A2, cs_off[sl], 1024),
                                in_=io[ctab][pp * 128:(pp + 1) * 128, :]),
                    s.dma_start(out=a_f32(r.A2, cs_off[sl] + 2048, 1024),
                                in_=io[stab][pp * 128:(pp + 1) * 128, :])]
        ev_cs[pp] = ph.do("sync", ldc, after=after_c, mark=True, dma="dL" + str(pp % 2), n=2)

        def mmk(te, pp=pp, sl=sl):
            last = None
            for half, psum in ((0, r.PSB), (1, r.PSA)):
                wt = r.A2[:, wk_slot[sl] + half * 2048:wk_slot[sl] + (half + 1) * 2048]
                for tg in range(2):
                    for kc in range(DC):
                        last = te.matmul(psum[:, tg * 512:(tg + 1) * 512],
                                         wt[:, kc * 128:(kc + 1) * 128],
                                         xn_chunk(r, kc)[:, tg * 512:(tg + 1) * 512],
                                         start=(kc == 0), stop=(kc == DC - 1))
            return last
        after_m = [xn_ev, ev_w[pp]]
        if pp >= 1:
            after_m.append(ev_cp[pp - 1])
        else:
            after_m += list(after0)
        ev_mm[pp] = ph.do("tensor", mmk, after=after_m, mark=True)

        def cp_(a):
            a.copy(tmpE[:, :], r.PSB[:, :])
            return a.copy(tmpO[:, :], r.PSA[:, 0:1024])
        after_c2 = [ev_mm[pp]]
        if pp >= 1:
            after_c2.append(ev_rope[pp - 1])
        ev_cp[pp] = ph.do("scalar", cp_, after=after_c2, mark=True)

        if dst_kind == "qf":
            d_e = qf_chunk(r, pp)
            d_o = qf_chunk(r, pp + 8)
        else:
            d_e = kst[sl][:, 0:1024]
            d_o = kst[sl][:, 1024:2048]

        def rope_t(v, pp=pp, sl=sl, d_e=d_e):
            cos = a_f32(r.A2, cs_off[sl], 1024)
            sin = a_f32(r.A2, cs_off[sl] + 2048, 1024)
            v.tensor_tensor(tmp1[:, :], tmpE[:, :], cos, ALU.mult)
            return v.tensor_tensor(tmp2[:, :], tmpO[:, :], sin, ALU.mult)
        after_r = [ev_cp[pp], ev_cs[pp]]
        if dst_kind == "dram" and pp >= 2:
            after_r.append(ev_st[pp - 2])
        e_t = ph.do("vector", rope_t, after=after_r, mark=True)

        e_t2 = ph.do("vector", lambda v, sl=sl, d_e=d_e: [
            v.tensor_sub(d_e, tmp1[:, :], tmp2[:, :]),
            v.tensor_tensor(tmp1[:, :], tmpE[:, :], a_f32(r.A2, cs_off[sl] + 2048, 1024), ALU.mult),
            v.tensor_tensor(tmp2[:, :], tmpO[:, :], a_f32(r.A2, cs_off[sl], 1024), ALU.mult)][-1],
            after=[e_t.h()], mark=True)
        ev_rope[pp] = ph.do("vector", lambda v, d_o=d_o: v.tensor_add(
            d_o, tmp1[:, :], tmp2[:, :]), after=[e_t2.h()], mark=True)

        if dst_kind == "dram":
            def stk(s, pp=pp, sl=sl):
                return [s.dma_start(out=dram_k[pp * 128:(pp + 1) * 128, :],
                                    in_=kst[sl][:, 0:1024]),
                        s.dma_start(out=dram_k[(pp + 8) * 128:(pp + 9) * 128, :],
                                    in_=kst[sl][:, 1024:2048])]
            ev_st[pp] = ph.do("scalar", stk, after=[ev_rope[pp]], mark=True, dma="dS" + str(pp % 2), n=2)
    ph.flush()
    last = ev_st[7] if dst_kind == "dram" else ev_rope[7]
    return last, (ev_st if dst_kind == "dram" else ev_rope)


def emit_v_l0(p, r, io, wname, dram_v, xn_ev, after0, use_psa=True, vst_off=(24576, 26624)):
    ph = Phase(p)
    vst = [r.A2[:, vst_off[0]:vst_off[0] + 2048], r.A2[:, vst_off[1]:vst_off[1] + 2048]]
    wslot = [0, 8192, 16384, 24576]
    rot = [r.PSC, r.PSB[:, 0:512], r.PSB[:, 512:1024]]
    if use_psa:
        rot += [r.PSA[:, 0:512], r.PSA[:, 512:1024], r.PSA[:, 1024:1536]]
    NR = len(rot)
    ev_w = [None] * 4
    ev_mm = {}
    ev_evac = {}
    ev_st = [None] * TT
    def ldall(g):
        return [g.dma_start(
            out=r.A1[:, wslot[dvg]:wslot[dvg] + 8192],
            in_=io[wname][dvg * 128:(dvg + 1) * 128, :]) for dvg in range(4)]
    ev_wall = ph.do("gpsimd", ldall, after=list(after0), mark=True, dma="dW0", n=4)
    for dvg in range(4):
        ev_w[dvg] = ev_wall
    u = 0
    for t in range(TT):
        for dvg in range(4):
            psum = rot[u % NR]

            def mmv(te, t=t, dvg=dvg, psum=psum):
                last = None
                wv = r.A1[:, wslot[dvg]:wslot[dvg] + 8192]
                for kc in range(DC):
                    last = te.matmul(psum[:, 0:512],
                                     xn_chunk(r, kc)[:, t * 128:(t + 1) * 128],
                                     wv[:, kc * 512:(kc + 1) * 512],
                                     start=(kc == 0), stop=(kc == DC - 1))
                return last
            after_m = [xn_ev, ev_w[dvg]]
            if u >= NR:
                after_m.append(ev_evac[u - NR])
            else:
                after_m += list(after0)
            ev_mm[(t, dvg)] = ph.do("tensor", mmv, after=after_m, mark=True)
            after_e = [ev_mm[(t, dvg)]]
            if t >= 2 and dvg == 0:
                after_e.append(ev_st[t - 2])
            ev_evac[u] = ph.do("scalar", lambda a, t=t, dvg=dvg, psum=psum: a.copy(
                vst[t % 2][:, dvg * 512:(dvg + 1) * 512], psum[:, 0:512]),
                after=after_e, mark=True)
            u += 1
        ev_st[t] = ph.do("scalar", lambda s, t=t: s.dma_start(
            out=dram_v[t * 128:(t + 1) * 128, :], in_=vst[t % 2][:, :]),
            after=[ev_evac[u - 1].h()], mark=True, dma="dS" + str(t % 2))
    ph.flush()
    return ev_st[TT - 1], ev_st


def emit_ag(p, dram_in_list, dram_out_list, groups, after):
    ph = Phase(p)
    evs = []
    for di, do in zip(dram_in_list, dram_out_list):
        evs.append(ph.do("gpsimd", lambda g, di=di, do=do: g.collective_compute(
            "AllGather", ALU.bypass, replica_groups=groups, ins=[di.ap()], outs=[do.ap()]),
            after=after, mark=True, cc=True))
        after = ()
    ph.flush()
    return evs


def emit_mlp_stage(p, r, io, wname, n_in_chunks, n_out_tiles, src_chunk_fn, dst_arena,
                   bias_col0, xn_ev, after0, strip_slots, strip_arena):
    ph = Phase(p)
    ssz = n_in_chunks * 128
    rot = [r.PSC, r.PSB[:, 0:512], r.PSB[:, 512:1024],
           r.PSA[:, 0:512], r.PSA[:, 512:1024], r.PSA[:, 1024:1536]]
    ev_w = [None] * n_out_tiles
    ev_mm = {}
    ev_evac = {}
    u = 0
    for ht in range(n_out_tiles):
        sl = ht % 2
        after_w = list(after0) if ht < 2 else [ev_mm[(ht - 2, 1)]]
        ev_w[ht] = ph.do("gpsimd", lambda g, ht=ht, sl=sl: g.dma_start(
            out=strip_arena[:, strip_slots[sl]:strip_slots[sl] + ssz],
            in_=io[wname][ht * 128:(ht + 1) * 128, :]),
            after=after_w, mark=True, dma="dW" + str(ht % 2))
        for tg in range(2):
            psum = rot[u % 6]

            def mmh(te, ht=ht, tg=tg, sl=sl, psum=psum):
                last = None
                wt = strip_arena[:, strip_slots[sl]:strip_slots[sl] + ssz]
                for kc in range(n_in_chunks):
                    last = te.matmul(psum[:, 0:512],
                                     wt[:, kc * 128:(kc + 1) * 128],
                                     src_chunk_fn(kc)[:, tg * 512:(tg + 1) * 512],
                                     start=(kc == 0), stop=(kc == n_in_chunks - 1))
                return last
            after_m = [xn_ev, ev_w[ht]]
            if u >= 6:
                after_m.append(ev_evac[u - 6])
            else:
                after_m += list(after0)
            ev_mm[(ht, tg)] = ph.do("tensor", mmh, after=after_m, mark=True)
            ev_evac[u] = ph.do("scalar", lambda a, ht=ht, tg=tg, psum=psum: a.activation(
                dst_arena[:, ht * 1024 + tg * 512:ht * 1024 + (tg + 1) * 512],
                psum[:, 0:512], AF.Relu, bias=r.B12[:, bias_col0 + ht:bias_col0 + ht + 1]),
                after=[ev_mm[(ht, tg)]], mark=True)
            u += 1
    ph.flush()
    return ev_evac[u - 1]


def emit_ffn_final_l0(p, r, io, dram, h2_ev, after0):
    ph = Phase(p)
    x0st = [a_f32(r.A1, 0, 4096), a_f32(r.A1, 8192, 4096)]
    xmst = [a_f32(r.A1, 16384, 4096), a_f32(r.A1, 24576, 4096)]
    wslot = [0, 8192]
    rot = [r.PSC, r.PSB[:, 0:512], r.PSB[:, 512:1024],
           r.PSA[:, 0:512], r.PSA[:, 512:1024], r.PSA[:, 1024:1536]]
    ev_x0 = [None] * 4
    ev_w = {}
    ev_mm = {}
    ev_evac = {}
    ev_out = [None] * 4
    u = 0
    for dout in range(4):
        after_x = list(after0)
        if dout >= 2:
            after_x.append(ev_evac[(dout - 2, TT - 1)])
        ev_x0[dout] = ph.do("sync", lambda s, dout=dout: s.dma_start(
            out=x0st[dout % 2][:, :],
            in_=io["x0"][:, dout * 512:(dout + 1) * 512].rearrange("(tc p) m -> p tc m", p=128)),
            after=after_x, mark=True, dma="dL" + str(dout % 2))
        for half in range(2):
            s2 = dout * 2 + half
            after_w = list(after0) if s2 < 2 else [ev_mm[(dout - 1, half, TT - 1)]]
            ev_w[(dout, half)] = ph.do("gpsimd", lambda g, dout=dout, half=half: g.dma_start(
                out=r.QF[:, wslot[half]:wslot[half] + 8192],
                in_=io["w3T"][dout * 128:(dout + 1) * 128, half * 8192:(half + 1) * 8192]),
                after=after_w, mark=True, dma="dW" + str(half))
        for t in range(TT):
            psum = rot[u % 6]

            def mmf(te, dout=dout, t=t, psum=psum):
                last = None
                for half in range(2):
                    wt = r.QF[:, wslot[half]:wslot[half] + 8192]
                    for kc in range(16):
                        hc = half * 16 + kc
                        last = te.matmul(psum[:, 0:512],
                                         r.A2[:, hc * 1024 + t * 128:hc * 1024 + (t + 1) * 128],
                                         wt[:, kc * 512:(kc + 1) * 512],
                                         start=(half == 0 and kc == 0), stop=False)
                last = te.matmul(psum[:, 0:512], r.ONES[0:1, 0:128],
                                 r.B3[0:1, dout * 512:(dout + 1) * 512],
                                 start=False, stop=True)
                return last
            after_m = [h2_ev, ev_w[(dout, 0)], ev_w[(dout, 1)]]
            if u >= 6:
                after_m.append(ev_evac[(dout - (1 if t < 6 else 0), (t - 6) % TT)])
            else:
                after_m += list(after0)
            ev_mm[(dout, 0, t)] = ev_mm[(dout, 1, t)] = ph.do("tensor", mmf, after=after_m, mark=True)
            ev_evac[(dout, t)] = ph.do("vector", lambda v, dout=dout, t=t, psum=psum: v.tensor_tensor(
                xmst[dout % 2][:, t * 512:(t + 1) * 512], psum[:, 0:512],
                x0st[dout % 2][:, t * 512:(t + 1) * 512], ALU.add),
                after=[ev_mm[(dout, 0, t)], ev_x0[dout]], mark=True)
            u += 1
        after_o = [ev_evac[(dout, TT - 1)]]
        if dout >= 2:
            after_o.append(ev_out[dout - 2])
        ev_out[dout] = ph.do("sync", lambda s, dout=dout: s.dma_start(
            out=dram["dXm"][:, dout * 512:(dout + 1) * 512].rearrange("(tc p) m -> p tc m", p=128),
            in_=xmst[dout % 2][:, :].rearrange("p (tc m) -> p tc m", tc=TT)),
            after=after_o, mark=True, dma="dL" + str(2 + dout % 2))
    ph.flush()
    return ev_out[3], ev_mm[(3, 1, TT - 1)]


def emit_attn_l0(p, r, dram, io, cc_k, cc_v, q_ev, after_k, after_v, after0):
    ph = Phase(p)
    mask_sl = [a_f32(r.XN, 0, 1152), a_f32(r.XN, 2304, 1152)]
    probs = r.XN[:, 4608:6656]
    probsT = r.XN[:, 6656:8704]
    avt = r.XN[:, 8704:10752]
    avfm = r.XN[:, 10752:12800]
    ST = r.STATS

    def ldk(s):
        out = []
        for c in range(DC):
            out.append(s.dma_start(out=r.A1[:, c * 2048:c * 2048 + 1024],
                                   in_=dram["gK"][c * 128:(c + 1) * 128, :]))
            out.append(s.dma_start(out=r.A1[:, c * 2048 + 1024:c * 2048 + 2048],
                                   in_=dram["gK"][2048 + c * 128:2048 + (c + 1) * 128, :]))
        return out
    ev_kld = ph.do("scalar", ldk, after=[cc_k] + list(after_k), mark=True, dma="dS0", n=32)

    def ldv(s):
        out = []
        for c in range(DC):
            out.append(s.dma_start(out=r.A2[:, c * 2048:(c + 1) * 2048],
                                   in_=dram["gV"][c * 128:(c + 1) * 128, :]))
        return out
    ev_vld = ph.do("scalar", ldv, after=[cc_v] + list(after_v), mark=True, dma="dS1", n=16)
    if ATTN_DBG is not None and "ka" in ATTN_DBG:
        e_ka = ph.do("sync", lambda s: s.dma_start(out=ATTN_DBG["ka"], in_=r.A1[:, 0:4096]),
                     after=[ev_kld], mark=True, dma="dL3")
        ATTN_DBG["kaev"] = e_ka

    ev_mask = [None] * TT
    ev_madd = [None] * TT
    ev_max = [None] * TT
    ev_exp = [None] * TT
    ev_rec = [None] * TT
    ev_ptev = [[None, None] for _ in range(TT)]
    ev_avev = [[None, None] for _ in range(TT)]
    ev_avfmev = [[None, None] for _ in range(TT)]
    ev_avdma = [None] * TT
    ev_sc = [None] * TT

    for j in range(TT):
        w = W0[j]
        nr = w // 128
        ngr = (w + 511) // 512
        after_mk = [q_ev] + list(after0)
        if j >= 2:
            after_mk.append(ev_madd[j - 2])
        ev_mask[j] = ph.do("scalar", lambda s, j=j: s.dma_start(
            out=mask_sl[j % 2][:, :], in_=io["mask0"][j, :, :]),
            after=after_mk, mark=True, dma="dS" + str(2 + j % 2))

        def msc(te, j=j, w=w, ngr=ngr):
            last = None
            for g in range(ngr):
                n = min(512, w - g * 512)
                for c in range(DC):
                    last = te.matmul(r.PSA[:, g * 512:g * 512 + n],
                                     qf_chunk(r, c)[:, j * 128:(j + 1) * 128],
                                     r.A1[:, c * 2048 + g * 512:c * 2048 + g * 512 + n],
                                     start=(c == 0), stop=(c == DC - 1))
            return last
        after_s = [ev_kld, q_ev] + list(after0)
        if j >= 1:
            after_s.append(ev_exp[j - 1])
        if ATTN_DBG is not None and j == 1 and "evs" in ATTN_DBG:
            after_s += ATTN_DBG["evs"]
        ev_sc[j] = ph.do("tensor", msc, after=after_s, mark=True)

        ev_madd[j] = ph.do("vector", lambda v, j=j, w=w: v.tensor_tensor(
            r.PSA[:, w - 1152:w], r.PSA[:, w - 1152:w], mask_sl[j % 2][:, :], ALU.add),
            after=[ev_sc[j], ev_mask[j]], mark=True)
        if ATTN_DBG is not None and j == 0:
            scr = r.XN[:, 12800:14848]
            e_cp = ph.do("vector", lambda v: v.tensor_copy(a_f32(r.XN, 12800, 1024)[:, 0:1024], r.PSA[:, 0:1024]),
                         after=[ev_madd[j].h()], mark=True)
            e_cp2 = ph.do("vector", lambda v: v.tensor_copy(a_f32(r.XN, 14848, 768)[:, 0:128], r.PSA[:, 1024:1152]),
                         after=[e_cp.h()], mark=True)
            e_d1 = ph.do("sync", lambda s: s.dma_start(out=ATTN_DBG["s0"][:, 0:1024], in_=a_f32(r.XN, 12800, 1024)),
                         after=[e_cp2], mark=True, dma="dL2")
            e_d1b = ph.do("sync", lambda s: s.dma_start(out=ATTN_DBG["s0"][:, 1024:1152], in_=a_f32(r.XN, 14848, 768)[:, 0:128]),
                         after=[e_cp2], mark=True, dma="dL2")
            ATTN_DBG["evs"] = [e_d1, e_d1b]
        ev_max[j] = ph.do("vector", lambda v, j=j, w=w: v.reduce_max(
            ST[:, 8:9], r.PSA[:, 0:w], axis=AX.X, negate=True),
            after=[ev_madd[j].h()], mark=True)
        ev_exp[j] = ph.do("scalar", lambda a, j=j, w=w: a.activation(
            probs[:, 0:w], r.PSA[:, 0:w], AF.Exp, bias=ST[:, 8:9], accum_out=ST[:, 9:10]),
            after=[ev_max[j]], mark=True)
        ev_rec[j] = ph.do("vector", lambda v: v.reciprocal(ST[:, 10:11], ST[:, 9:10]),
                          after=[ev_exp[j]], mark=True)
        if ATTN_DBG is not None and j == 0:
            e_d2 = ph.do("sync", lambda s: s.dma_start(out=ATTN_DBG["p0"], in_=probs[:, 0:1152]),
                         after=[ev_exp[j]], mark=True, dma="dL2")
            e_d3 = ph.do("sync", lambda s: s.dma_start(out=ATTN_DBG["st0"], in_=ST[:, :]),
                         after=[ev_rec[j]], mark=True, dma="dL2")
            ATTN_DBG["evs"] += [e_d2, e_d3]
        if ATTN_DBG is not None and j == 1:
            pass
        nrd = 2
        for rd in range(nrd):
            n8 = 8 if rd == 0 else nr - 8
            after_t = [ev_exp[j]]
            if rd == 1:
                after_t.append(ev_ptev[j][0])
            elif j >= 1:
                after_t.append(ev_avfmev[j - 1][1])

            def tpp(te, j=j, rd=rd, n8=n8):
                last = None
                for i8 in range(n8):
                    c = rd * 8 + i8
                    last = te.transpose(r.PST[:, i8 * 128:(i8 + 1) * 128],
                                        probs[:, c * 128:(c + 1) * 128], r.IDENT[:, :])
                return last
            ev_tp = ph.do("tensor", tpp, after=after_t, mark=True)
            ev_ptev[j][rd] = ph.do("vector", lambda v, rd=rd, n8=n8: v.tensor_copy(
                probsT[:, rd * 1024:rd * 1024 + n8 * 128], r.PST[:, 0:n8 * 128]),
                after=[ev_tp], mark=True)
        for dvh in range(2):
            def mav(te, j=j, nr=nr, dvh=dvh):
                last = None
                for i2 in range(2):
                    for c in range(nr):
                        last = te.matmul(r.PSB[:, i2 * 512:(i2 + 1) * 512],
                                         probsT[:, c * 128:(c + 1) * 128],
                                         r.A2[:, c * 2048 + dvh * 1024 + i2 * 512:
                                              c * 2048 + dvh * 1024 + i2 * 512 + 512],
                                         start=(c == 0), stop=(c == nr - 1))
                return last
            after_av = [ev_ptev[j][1], ev_vld]
            if dvh == 1:
                after_av.append(ev_avev[j][0])
            elif j >= 1:
                after_av.append(ev_avev[j - 1][1])
            else:
                after_av += list(after0)
            ev_avmm = ph.do("tensor", mav, after=after_av, mark=True)
            ev_avev[j][dvh] = ph.do("scalar", lambda a, j=j, dvh=dvh: a.activation(
                avt[:, dvh * 1024:(dvh + 1) * 1024], r.PSB[:, :], AF.Identity,
                scale=ST[:, 10:11]),
                after=[ev_avmm, ev_rec[j]], mark=True)
        for rd in range(2):
            after_t = [ev_avev[j][1], ev_ptev[j][1]]
            if rd == 1:
                after_t.append(ev_avfmev[j][0])
            if j >= 1:
                after_t.append(ev_avdma[j - 1])

            def tpa(te, j=j, rd=rd):
                last = None
                for i8 in range(8):
                    c = rd * 8 + i8
                    last = te.transpose(r.PST[:, i8 * 128:(i8 + 1) * 128],
                                        avt[:, c * 128:(c + 1) * 128], r.IDENT[:, :])
                return last
            ev_tpa = ph.do("tensor", tpa, after=after_t, mark=True)
            ev_avfmev[j][rd] = ph.do("vector", lambda v, rd=rd: v.tensor_copy(
                avfm[:, rd * 1024:(rd + 1) * 1024], r.PST[:, :]),
                after=[ev_tpa], mark=True)
        ev_avdma[j] = ph.do("scalar", lambda s, j=j: s.dma_start(
            out=dram["dAV"][:, j * 128:(j + 1) * 128].rearrange("(c p) m -> p c m", p=128),
            in_=avfm[:, :].rearrange("p (c m) -> p c m", c=16)),
            after=[ev_avfmev[j][1]], mark=True, dma="dSA")
    ph.flush()
    return ev_avdma[TT - 1]


def emit_out_proj(p, r, dram, io, wkey, x_in_dram, out_dram, av_dma_ev, after0):
    ph = Phase(p)
    avsl = [r.A1[:, 0:2048], r.A1[:, 2048:4096]]
    xst = [a_f32(r.A1, 4096, 4096), a_f32(r.A1, 12288, 4096)]
    ost = [a_f32(r.QF, 0, 4096), a_f32(r.QF, 8192, 4096)]
    wslot = [16384, 24576]
    rot = [r.PSC, r.PSB[:, 0:512], r.PSB[:, 512:1024],
           r.PSA[:, 0:512], r.PSA[:, 512:1024], r.PSA[:, 1024:1536]]
    ev_w = [None] * 4
    ev_x = [None] * 4
    ev_av = [None] * TT
    ev_mm = {}
    ev_evac = {}
    ev_out = [None] * 4
    u = 0
    for t in range(TT):
        after_a = [av_dma_ev] + (list(after0) if t < 2 else [])
        pass
    def ldav(s):
        out = []
        for t in range(TT):
            out.append(s.dma_start(
                out=r.A1[:, t * 2048:(t + 1) * 2048].rearrange("p (c m) -> p c m", c=16),
                in_=dram["dAV"][:, t * 128:(t + 1) * 128].rearrange("(c p) m -> p c m", p=128)))
        return out
    ev_avall = ph.do("scalar", ldav, after=[av_dma_ev] + list(after0), mark=True, dma="dS0", n=8)
    xst = [a_f32(r.A1, 16384, 4096), a_f32(r.A1, 24576, 4096)]
    for dout in range(4):
        after_w = list(after0) if dout < 2 else [ev_mm[(dout - 2, TT - 1)]]
        ev_w[dout] = ph.do("gpsimd", lambda g, dout=dout: g.dma_start(
            out=r.A2[:, wslot[dout % 2]:wslot[dout % 2] + 8192],
            in_=io[wkey][dout * 128:(dout + 1) * 128, :]),
            after=after_w, mark=True, dma="dW" + str(dout % 2))
        after_x = list(after0)
        if dout >= 2:
            after_x.append(ev_evac[(dout - 2, TT - 1)])
        ev_x[dout] = ph.do("sync", lambda s, dout=dout: s.dma_start(
            out=xst[dout % 2][:, :],
            in_=x_in_dram[:, dout * 512:(dout + 1) * 512].rearrange("(tc p) m -> p tc m", p=128)),
            after=after_x, mark=True, dma="dL" + str(dout % 2))
        for t in range(TT):
            psum = rot[u % 6]

            def mmo(te, dout=dout, t=t, psum=psum):
                last = None
                wt = r.A2[:, wslot[dout % 2]:wslot[dout % 2] + 8192]
                for c in range(DC):
                    last = te.matmul(psum[:, 0:512],
                                     r.A1[:, t * 2048 + c * 128:t * 2048 + (c + 1) * 128],
                                     wt[:, c * 512:(c + 1) * 512],
                                     start=(c == 0), stop=(c == DC - 1))
                return last
            after_m = [ev_avall, ev_w[dout]]
            if u >= 6:
                after_m.append(ev_evac[(dout - (1 if t < 6 else 0), (t - 6) % TT)])
            else:
                after_m += list(after0)
            ev_mm[(dout, t)] = ph.do("tensor", mmo, after=after_m, mark=True)
            after_e = [ev_mm[(dout, t)], ev_x[dout]]
            if dout >= 2 and t == 0:
                after_e.append(ev_out[dout - 2])
            ev_evac[(dout, t)] = ph.do("vector", lambda v, dout=dout, t=t, psum=psum: v.tensor_tensor(
                ost[dout % 2][:, t * 512:(t + 1) * 512], psum[:, 0:512],
                xst[dout % 2][:, t * 512:(t + 1) * 512], ALU.add),
                after=after_e, mark=True)
            u += 1
        ev_out[dout] = ph.do("sync", lambda s, dout=dout: s.dma_start(
            out=out_dram[:, dout * 512:(dout + 1) * 512].rearrange("(tc p) m -> p tc m", p=128),
            in_=ost[dout % 2][:, :].rearrange("p (tc m) -> p tc m", tc=TT)),
            after=[ev_evac[(dout, TT - 1)]], mark=True, dma="dL" + str(2 + dout % 2))
    ph.flush()
    return ev_out[3], ev_out



def declare_io_l0(p, with_out=True):
    nc = p.nc
    io = {}
    def inp(name, shape):
        io[name] = nc.dram_tensor(name, shape, F32, kind="ExternalInput").ap()
    inp("x0", [T, D])
    inp("wqp", [D, D]); inp("wkp", [D, D])
    inp("wv", [512, 8192]); inp("wo", [512, 8192])
    inp("w1T", [4096, D]); inp("w2T", [4096, 4096]); inp("w3T", [512, 16384])
    inp("b12", [128, 64]); inp("b3", [1, D])
    inp("selc", [128, 2]); inp("selr", [2, 128])
    inp("cosq", [1024, 1024]); inp("sinq", [1024, 1024])
    inp("cosk", [1024, 1024]); inp("sink", [1024, 1024])
    inp("mask0", [TT, 128, 1152])
    if with_out:
        io["x1o"] = nc.dram_tensor("x1o", [T, D], F32, kind="ExternalOutput").ap()
    return io


def declare_dram_shared(p):
    nc = p.nc
    d = {}
    d["dKin"] = nc.dram_tensor("dKin", [D, T], BF16).ap()
    d["dVin"] = nc.dram_tensor("dVin", [T, D], BF16).ap()
    d["gK"] = nc.dram_tensor("gK", [2 * D, T], BF16).ap()
    d["gV"] = nc.dram_tensor("gV", [S, D], BF16).ap()
    d["dXm"] = nc.dram_tensor("dXm", [T, D], F32).ap()
    d["dAV"] = nc.dram_tensor("dAV", [D, T], BF16).ap()
    return d


PAIRS = [[0, 1], [2, 3], [4, 5], [6, 7]]
ALL8 = [[0, 1, 2, 3, 4, 5, 6, 7]]


def emit_l0_body(p, r, io, dram, x_in_ap, x1_out_ap, ev0):
    ev_xn = emit_ln(p, r, x_in_ap, ev0)
    ev_k, _ = emit_rope_proj(p, r, io, "wkp", "cosk", "sink", False, "dram",
                             dram["dKin"], ev_xn, after0=ev0)
    ev_v, _ = emit_v_l0(p, r, io, "wv", dram["dVin"], ev_xn, after0=[ev_k])
    ph = Phase(p)
    cc_k = ph.do("gpsimd", lambda g: g.collective_compute(
        "AllGather", ALU.bypass, replica_groups=PAIRS,
        ins=[dram["dKin"]], outs=[dram["gK"]]), after=[ev_v], mark=True, cc=True)
    cc_v = ph.do("gpsimd", lambda g: g.collective_compute(
        "AllGather", ALU.bypass, replica_groups=PAIRS,
        ins=[dram["dVin"]], outs=[dram["gV"]]), mark=True, cc=True)
    ph.flush()
    ev_h1 = emit_mlp_stage(p, r, io, "w1T", 16, 32, lambda kc: xn_chunk(r, kc),
                           r.A1, 0, ev_xn, [ev_k, ev_v], [0, 2048], r.A2)
    ev_h2 = emit_mlp_stage(p, r, io, "w2T", 32, 32,
                           lambda kc: r.A1[:, kc * 1024:(kc + 1) * 1024],
                           r.A2, 32, ev_h1, [ev_h1], [0, 4096], r.QF)
    ev_xm, ev_xm_mm = emit_ffn_final_l0(p, r, io, dram, ev_h2, [ev_h2])
    ev_q, _ = emit_rope_proj(p, r, io, "wqp", "cosq", "sinq", True, "qf",
                             None, ev_xn, after0=[ev_xm, ev_xm_mm])
    ev_av = emit_attn_l0(p, r, dram, io, cc_k, cc_v, ev_q,
                         after_k=[ev_h2, ev_xm], after_v=[ev_xm_mm, ev_q], after0=[ev_q, ev_xm])
    ev_x1, _ = emit_out_proj(p, r, dram, io, "wo", dram["dXm"], x1_out_ap,
                             ev_av, [ev_av])
    return ev_x1


def emit_tail_wait(p):
    ph = Phase(p)

    def w_(s):
        for nm in ("dL0", "dL1", "dL2", "dL3", "dW0", "dW1", "dW2", "dW3",
                   "dS0", "dS1", "dS2", "dS3", "dS4", "dS5", "dSA", "cc"):
            s.wait_ge(p.semh[nm], p.cnt[nm])
        return s.nop()
    ph.do("sync", w_)
    ph.flush()


def build_l0_program():
    p = Prog()
    r = alloc_resources(p)
    io = declare_io_l0(p)
    dram = declare_dram_shared(p)
    ev_c, ev_s = emit_consts(p, r, io)
    emit_l0_body(p, r, io, dram, io["x0"], io["x1o"], [ev_c, ev_s])
    emit_tail_wait(p)
    return p, io



def _rope_tables(h):
    inv = (10000.0 ** (-(np.arange(1024, dtype=np.float64) / 1024.0))).astype(np.float32)
    pos = (h * 1024 + np.arange(1024, dtype=np.float64)).astype(np.float32)
    ang = pos[:, None].astype(np.float32) * inv[None, :]
    cos = np.cos(ang.astype(np.float32)).astype(np.float32)
    sin = np.sin(ang.astype(np.float32)).astype(np.float32)
    return np.ascontiguousarray(cos.T), np.ascontiguousarray(sin.T)


def _masks_l0(h):
    m = np.zeros((TT, 128, 1152), np.float32)
    for j in range(TT):
        w = W0[j]
        kc = np.arange(w - 1152, w)[None, :]
        pos = (h * 1024 + j * 128 + np.arange(128))[:, None]
        m[j] = np.where(kc > pos, np.float32(-1e30), np.float32(0.0))
    return m


def stripify(W, tw):
    R, C = W.shape
    return np.ascontiguousarray(
        W.reshape(R // 128, 128, C // tw, tw).transpose(2, 1, 0, 3)
        .reshape(C // tw * 128, (R // 128) * tw))


def prep_shared_l0(inputs):
    perm = np.concatenate([np.arange(0, D, 2), np.arange(1, D, 2)])
    sh = {}
    sh["wqp"] = stripify(np.asarray(inputs["L0_attn_W_q"], np.float32)[:, perm], 128)
    sh["wkp"] = stripify(np.asarray(inputs["L0_attn_W_k"], np.float32)[:, perm], 128)
    sh["wv"] = stripify(np.asarray(inputs["L0_attn_W_v"], np.float32), 512)
    sh["wo"] = stripify(np.asarray(inputs["L0_attn_W_o"], np.float32), 512)
    sh["w1T"] = stripify(np.asarray(inputs["L0_ffn_W1"], np.float32).T, 128)
    sh["w2T"] = stripify(np.asarray(inputs["L0_ffn_W2"], np.float32).T, 128)
    sh["w3T"] = stripify(np.asarray(inputs["L0_ffn_W3"], np.float32).T, 512)
    b1 = np.asarray(inputs["L0_ffn_b1"], np.float32)
    b2 = np.asarray(inputs["L0_ffn_b2"], np.float32)
    b12 = np.zeros((128, 64), np.float32)
    b12[:, 0:32] = b1.reshape(32, 128).T
    b12[:, 32:64] = b2.reshape(32, 128).T
    sh["b12"] = b12
    sh["b3"] = np.asarray(inputs["L0_ffn_b3"], np.float32).reshape(1, D)
    selc = np.zeros((128, 2), np.float32)
    selc[0:64, 0] = 1.0
    selc[64:128, 1] = 1.0
    sh["selc"] = selc
    sh["selr"] = np.ascontiguousarray(selc.T)
    return sh


def make_in_maps_l0(inputs):
    x = np.asarray(inputs["L0_attn_q"], np.float32)
    sh = prep_shared_l0(inputs)
    maps = []
    for c in range(8):
        b, h = c // 2, c % 2
        m = dict(sh)
        m["x0"] = np.ascontiguousarray(x[b, h * 1024:(h + 1) * 1024, :])
        cosk, sink = _rope_tables(h)
        m["cosk"], m["sink"] = cosk, sink
        m["cosq"] = (cosk * np.float32(SC0)).astype(np.float32)
        m["sinq"] = (sink * np.float32(SC0)).astype(np.float32)
        m["mask0"] = _masks_l0(h)
        maps.append(m)
    return maps


_L0_CACHE = {}


def run_l0(inputs):
    if "p" not in _L0_CACHE:
        p, io = build_l0_program()
        _L0_CACHE["p"] = p
    p = _L0_CACHE["p"]
    maps = make_in_maps_l0(inputs)
    res = run_bass_kernel_spmd(p.nc, maps, core_ids=list(range(8)))
    out = np.zeros((4, S, D), np.float32)
    for c in range(8):
        b, h = c // 2, c % 2
        out[b, h * 1024:(h + 1) * 1024, :] = res.results[c]["x1o"]
    return out


def kernel_unfused(**inputs):
    x1 = run_l0(inputs)
    return run_l1(inputs, x1)



def emit_quant_tiles(p, r, io, wkey, dst_dram, tiles, st, after0):
    ph = Phase(p)
    C = 0.50048828125
    wb = [r.A2[:, 8192:10240], r.A2[:, 10240:12288]]
    absf = [r.A2[:, 12288:13312], r.A2[:, 13312:14336]]
    c1 = [a_f32(r.A2, 16384, 1024), a_f32(r.A2, 18432, 1024)]
    c2 = [a_f32(r.A2, 20480, 1024), a_f32(r.A2, 22528, 1024)]
    wqo = r.A2[:, 24576:26624]
    sclf = a_f32(r.A2, 26624, 2048)
    sclb = r.A2[:, 30720:32768]
    if "n" not in st:
        st.update(n=0, ld={}, wq={}, dma={}, half={})
    for tc, gtc in tiles:
        k = st["n"]
        sl = k % 2
        after_l = list(after0)
        if k >= 2:
            after_l.append(st["wq"][(k - 2, 1)])
        e_ld = ph.do("gpsimd", lambda g, tc=tc, sl=sl, wkey=wkey: g.dma_start(
            out=wb[sl][:, :], in_=io[wkey][tc * 128:(tc + 1) * 128, :]),
            after=after_l, mark=True, dma="dW" + str(2 + sl))
        for hf in range(2):
            psa = r.PSA[:, hf * 1024:(hf + 1) * 1024]
            wbh = wb[sl][:, hf * 1024:(hf + 1) * 1024]
            e_abs = ph.do("scalar", lambda a, hf=hf, wbh=wbh: a.activation(
                absf[hf][:, :], wbh, AF.Abs), after=[e_ld] + list(after0), mark=True)

            def gsum(te, hf=hf, psa=psa):
                lastm = None
                for g6 in range(2):
                    lastm = te.matmul(psa[0:2, g6 * 512:(g6 + 1) * 512], r.SELB[:, :],
                                      absf[hf][:, g6 * 512:(g6 + 1) * 512],
                                      start=True, stop=True)
                return lastm
            after_gs = [e_abs] + list(after0)
            if k >= 1:
                after_gs.append(st["wq"][(k - 1, hf)])
            elif (k, hf) in ((0, 1),):
                pass
            if (k, hf) != (0, 0) and (k, hf) != (0, 1):
                pass
            e_gs = ph.do("tensor", gsum, after=after_gs, mark=True)
            e_sc = ph.do("vector", lambda v, hf=hf, psa=psa: v.tensor_scalar(
                sclf[0:2, hf * 1024:(hf + 1) * 1024], psa[0:2, 0:1024],
                1.0 / 64.0, 1e-8, ALU.mult, ALU.max),
                after=[e_gs], mark=True)
            e_sb = ph.do("vector", lambda v, hf=hf: v.tensor_copy(
                sclb[0:2, hf * 1024:(hf + 1) * 1024],
                sclf[0:2, hf * 1024:(hf + 1) * 1024]),
                after=[e_sc.h()], mark=True)
            e_sf = ph.do("vector", lambda v, hf=hf: v.tensor_copy(
                sclf[0:2, hf * 1024:(hf + 1) * 1024],
                sclb[0:2, hf * 1024:(hf + 1) * 1024]),
                after=[e_sb.h()], mark=True)

            def bscl(te, hf=hf, psa=psa):
                lastm = None
                for g6 in range(2):
                    lastm = te.matmul(psa[:, g6 * 512:(g6 + 1) * 512], r.SELT[:, :],
                                      sclf[0:2, hf * 1024 + g6 * 512:hf * 1024 + (g6 + 1) * 512],
                                      start=True, stop=True)
                return lastm
            e_bs = ph.do("tensor", bscl, after=[e_sf, e_sc], mark=True)
            e_c1 = ph.do("vector", lambda v, hf=hf, wbh=wbh, psa=psa: v.scalar_tensor_tensor(
                c1[hf][:, :], psa[:, 0:1024], C, wbh, ALU.mult, ALU.is_lt),
                after=[e_bs], mark=True)
            e_c2 = ph.do("vector", lambda v, hf=hf, wbh=wbh, psa=psa: v.scalar_tensor_tensor(
                c2[hf][:, :], psa[:, 0:1024], -C, wbh, ALU.mult, ALU.is_gt),
                after=[e_c1], mark=True)
            e_q = ph.do("vector", lambda v, hf=hf: v.tensor_sub(
                c1[hf][:, :], c1[hf][:, :], c2[hf][:, :]),
                after=[e_c2.h()], mark=True)
            after_wq = [e_q.h()]
            if k >= 1 and hf == 0:
                after_wq.append(st["dma"][k - 1])
            st["wq"][(k, hf)] = ph.do("vector", lambda v, hf=hf, psa=psa: v.tensor_tensor(
                wqo[:, hf * 1024:(hf + 1) * 1024], c1[hf][:, :], psa[:, 0:1024], ALU.mult),
                after=after_wq, mark=True)
        st["dma"][k] = ph.do("scalar", lambda s, gtc=gtc, dst_dram=dst_dram: s.dma_start(
            out=dst_dram[gtc * 128:(gtc + 1) * 128, :], in_=wqo[:, :]),
            after=[st["wq"][(k, 0)], st["wq"][(k, 1)]], mark=True, dma="dS" + str(4 + k % 2))
        st["n"] = k + 1
    ph.flush()
    return st["dma"][st["n"] - 1]


def emit_k_l1(p, r, io, dram, xn_ev, after0, scale=None, dst_kind="dram"):
    ph = Phase(p)
    wslot = [0, 2048]
    kst = [r.A2[:, 4096:5120], r.A2[:, 5120:6144]]
    rot = [r.PSB[:, 0:512], r.PSB[:, 512:1024], r.PSC]
    wname = "wqT" if dst_kind == "qf" else "wkT"
    ev_w = [None] * DC
    ev_mm = {}
    ev_ev = {}
    ev_st = [None] * DC
    u = 0
    for ko in range(DC):
        sl = ko % 2
        after_w = list(after0) if ko < 2 else [ev_mm[(ko - 2, 1)]]
        ev_w[ko] = ph.do("gpsimd", lambda g, ko=ko, sl=sl: g.dma_start(
            out=r.A2[:, wslot[sl]:wslot[sl] + 2048],
            in_=io[wname][ko * 128:(ko + 1) * 128, :]),
            after=after_w, mark=True, dma="dW" + str(sl))
        for tg in range(2):
            psum = rot[u % 3]

            def mmk(te, ko=ko, tg=tg, sl=sl, psum=psum):
                lastm = None
                wt = r.A2[:, wslot[sl]:wslot[sl] + 2048]
                for kc in range(DC):
                    lastm = te.matmul(psum[:, 0:512],
                                      wt[:, kc * 128:(kc + 1) * 128],
                                      xn_chunk(r, kc)[:, tg * 512:(tg + 1) * 512],
                                      start=(kc == 0), stop=(kc == DC - 1))
                return lastm
            after_m = [xn_ev, ev_w[ko]]
            if u >= 3:
                after_m.append(ev_ev[u - 3])
            else:
                after_m += list(after0)
            ev_mm[(ko, tg)] = ph.do("tensor", mmk, after=after_m, mark=True)
            if dst_kind == "qf":
                dst = qf_chunk(r, ko)[:, tg * 512:(tg + 1) * 512]
            else:
                dst = kst[sl][:, tg * 512:(tg + 1) * 512]
            after_e = [ev_mm[(ko, tg)]]
            if dst_kind == "dram" and ko >= 2 and tg == 0:
                after_e.append(ev_st[ko - 2])
            ev_ev[u] = ph.do("scalar", lambda a, psum=psum, dst=dst: a.activation(
                dst, psum[:, 0:512], AF.Identity, scale=(scale if scale else 1.0)),
                after=after_e, mark=True)
            u += 1
        if dst_kind == "dram":
            ev_st[ko] = ph.do("scalar", lambda s, ko=ko, sl=sl: s.dma_start(
                out=dram["dKin"][ko * 128:(ko + 1) * 128, :], in_=kst[sl][:, :]),
                after=[ev_ev[u - 1].h()], mark=True, dma="dS" + str(sl))
    ph.flush()
    return (ev_st[DC - 1] if dst_kind == "dram" else ev_ev[u - 1]), ev_ev


def emit_gu_attn_l1(p, r, io, dram, xn_ev, q_ev, cc_kv, cc_q2, after0):
    ph = Phase(p)
    kh = [r.A1[:, 0:2048], r.A1[:, 2048:4096]]
    vh = [r.A1[:, 4096:6144], r.A1[:, 6144:8192]]
    probs = [r.A1[:, 8192:10240], r.A1[:, 25856:27904]]
    probsT = r.A1[:, 10240:12288]
    avst = r.A1[:, 12288:12544]
    avfmT = r.A1[:, 12544:13568]
    hsil = a_f32(r.A1, 13568, 512)
    sqscr = a_f32(r.A1, 14592, 512)
    dsc = a_f32(r.A1, 15616, 512)
    grs = a_f32(r.A1, 16640, 512)
    gsb = a_f32(r.A1, 27904, 512)
    usb = a_f32(r.A1, 28928, 512)
    hstage = [r.A1[:, 17664:21760], r.A1[:, 21760:25856]]
    gslot = [0, 16384]
    ST = r.STATS

    ev_gw = [None] * 16
    ev_gmm = {}
    ev_trio = {}
    ev_hev = {}
    ev_hdma = [None] * 16
    ev_kvld = [None] * 16
    ev_sc = {}
    ev_mx = {}
    ev_exp = {}
    ev_rec = {}
    ev_ptev = {}
    ev_avev = {}
    ev_avmm = {}
    ev_avtev = {}
    ev_avdma = [None] * 16
    ev_sq_last = None

    def gu_strip_load(g, gi, sl):
        outs = []
        for part, cg in ((0, gi), (1, gi + 16)):
            rb, cc_ = cg // 4, (cg % 4) * 512
            outs.append(g.dma_start(
                out=r.A2[:, gslot[sl] + part * 8192:gslot[sl] + (part + 1) * 8192].rearrange(
                    "p (kc m) -> p kc m", kc=16),
                in_=dram["qguO"][rb * 2048:(rb + 1) * 2048, cc_:cc_ + 512].rearrange(
                    "(kc p) m -> p kc m", p=128)))
        return outs

    def emit_gu_mm(gi, t, sl):
        def mmg(te, gi=gi, t=t, sl=sl):
            lastm = None
            for part, ps_ in ((0, r.PSB[:, 0:512]), (1, r.PSB[:, 512:1024])):
                wt = r.A2[:, gslot[sl] + part * 8192:gslot[sl] + (part + 1) * 8192]
                for kc in range(DC):
                    lastm = te.matmul(ps_, xn_chunk(r, kc)[:, t * 128:(t + 1) * 128],
                                      wt[:, kc * 512:(kc + 1) * 512],
                                      start=(kc == 0), stop=(kc == DC - 1))
            return lastm
        after_m = [xn_ev, ev_gw[gi]]
        if (gi, t) != (0, 0):
            pt = (gi, t - 1) if t else (gi - 1, TT - 1)
            after_m.append(ev_trio[pt])
        else:
            after_m += list(after0)
        ev_gmm[(gi, t)] = ph.do("tensor", mmg, after=after_m, mark=True)

    def emit_gu_evac(gi, t, sl):
        def trio(a):
            a.activation(hsil[:, :], r.PSB[:, 0:512], AF.Tanh, scale=0.5)
            a.activation(gsb[:, :], r.PSB[:, 0:512], AF.Identity, scale=0.5)
            return a.copy(usb[:, :], r.PSB[:, 512:1024])
        after_t3 = [ev_gmm[(gi, t)]]
        if (gi, t) != (0, 0):
            pt = (gi, t - 1) if t else (gi - 1, TT - 1)
            after_t3.append(ev_hev[pt])
        ev_trio[(gi, t)] = ph.do("scalar", trio, after=after_t3, mark=True)
        e_d = ph.do("gpsimd", lambda g: g.tensor_tensor(
            dsc[:, :], gsb[:, :], hsil[:, :], ALU.mult),
            after=[ev_trio[(gi, t)]], mark=True)
        e_gr = ph.do("gpsimd", lambda g: g.tensor_tensor(
            grs[:, :], dsc[:, :], gsb[:, :], ALU.add), after=[e_d.h()], mark=True)
        after_h = [e_gr.h()]
        if gi >= 2 and t == 0:
            after_h.append(ev_hdma[gi - 2])
        e_h = ph.do("gpsimd", lambda g, t=t, sl=sl: g.tensor_tensor(
            hstage[sl][:, t * 512:(t + 1) * 512], grs[:, :], usb[:, :], ALU.mult),
            after=after_h, mark=True)
        ev_hev[(gi, t)] = e_h

    def emit_gu_sq(gi, t, sl):
        return ph.do("vector", lambda v, gi=gi, t=t, sl=sl: v.scalar_tensor_tensor(
            sqscr[:, :], hstage[sl][:, t * 512:(t + 1) * 512], 1.0,
            hstage[sl][:, t * 512:(t + 1) * 512], ALU.mult, ALU.mult,
            accum_out=r.SSQ[:, t * 16 + gi:t * 16 + gi + 1]),
            after=[ev_hev[(gi, t)]], mark=True)

    def emit_attn_a(hh, j, sl):
        pb = probs[(hh * TT + j) % 2]
        after_s = [ev_kvld[hh], q_ev]
        if (hh, j) != (0, 0):
            pt = (hh, j - 1) if j else (hh - 1, TT - 1)
            after_s.append(ev_exp[pt])
        else:
            after_s += list(after0)
        e_pm = None
        for g6 in range(4):
            e_m = ph.do("tensor", lambda te, hh=hh, j=j, sl=sl, g6=g6: te.matmul(
                r.PSA[:, g6 * 512:(g6 + 1) * 512],
                qf_chunk(r, hh)[:, j * 128:(j + 1) * 128],
                kh[sl][:, g6 * 512:(g6 + 1) * 512],
                start=True, stop=True),
                after=after_s if g6 == 0 else [], mark=True)
            e_pm = ph.do("vector", lambda v, g6=g6: v.reduce_max(
                ST[:, 12 + g6:13 + g6], r.PSA[:, g6 * 512:(g6 + 1) * 512], axis=AX.X),
                after=[e_m], mark=True)
        ev_sc[(hh, j)] = e_m
        ev_mx[(hh, j)] = ph.do("vector", lambda v: v.reduce_max(
            ST[:, 8:9], ST[:, 12:16], axis=AX.X, negate=True),
            after=[e_pm.h()], mark=True)
        after_e = [ev_mx[(hh, j)]]
        s = hh * TT + j
        if s >= 2:
            p2 = ((s - 2) // TT, (s - 2) % TT)
            after_e.append(ev_ptev[(p2, 1)])
        ev_exp[(hh, j)] = ph.do("scalar", lambda a, pb=pb: a.activation(
            pb[:, :], r.PSA[:, 0:2048], AF.Exp, bias=ST[:, 8:9], accum_out=ST[:, 9:10]),
            after=after_e, mark=True)
    def emit_attn_rec(hh, j):
        s = hh * TT + j
        after_r = [ev_exp[(hh, j)]]
        if s >= 2:
            p2 = ((s - 2) // TT, (s - 2) % TT)
            after_r.append(ev_avev[p2])
        ev_rec[(hh, j)] = ph.do("vector", lambda v, hh=hh, j=j: v.reciprocal(
            ST[:, 10 + (hh * TT + j) % 2:11 + (hh * TT + j) % 2], ST[:, 9:10]),
            after=after_r, mark=True)

    def emit_attn_b(hh, j, sl):
        pb = probs[(hh * TT + j) % 2]
        for rd in range(2):
            after_t = [ev_exp[(hh, j)]]
            if rd == 1:
                after_t.append(ev_ptev[((hh, j), 0)])
            else:
                pt = (hh, j - 1) if j else (hh - 1, TT - 1)
                if (hh, j) != (0, 0):
                    after_t.append(ev_avtev[pt])

            def tpp(te, rd=rd, pb=pb):
                lastm = None
                for i8 in range(8):
                    c = rd * 8 + i8
                    lastm = te.transpose(r.PST[:, i8 * 128:(i8 + 1) * 128],
                                         pb[:, c * 128:(c + 1) * 128], r.IDENT[:, :])
                return lastm
            e_tp = ph.do("tensor", tpp, after=after_t, mark=True)
            ev_ptev[((hh, j), rd)] = ph.do("vector", lambda v, rd=rd: v.tensor_copy(
                probsT[:, rd * 1024:(rd + 1) * 1024], r.PST[:, :]),
                after=[e_tp], mark=True)

        def mav(te, hh=hh, sl=sl):
            lastm = None
            for c in range(16):
                lastm = te.matmul(r.PSC[:, 0:128], probsT[:, c * 128:(c + 1) * 128],
                                  vh[sl][:, c * 128:(c + 1) * 128],
                                  start=(c == 0), stop=(c == 15))
            return lastm
        after_av = [ev_ptev[((hh, j), 1)]]
        if (hh, j) != (0, 0):
            pt = (hh, j - 1) if j else (hh - 1, TT - 1)
            after_av.append(ev_avev[pt])
        else:
            after_av += list(after0)
        ev_avmm[(hh, j)] = ph.do("tensor", mav, after=after_av, mark=True)

    def emit_attn_c(hh, j):
        ev_avev[(hh, j)] = ph.do("scalar", lambda a, hh=hh, j=j: a.activation(
            avst[:, 0:128], r.PSC[:, 0:128], AF.Identity,
            scale=ST[:, 10 + (hh * TT + j) % 2:11 + (hh * TT + j) % 2]),
            after=[ev_avmm[(hh, j)], ev_rec[(hh, j)]], mark=True)

        def tpa(te):
            return te.transpose(r.PST[:, 0:128], avst[:, 0:128], r.IDENT[:, :])
        after_ta = [ev_avev[(hh, j)], ev_ptev[((hh, j), 1)]]
        if hh >= 1 and j == 0:
            after_ta.append(ev_avdma[hh - 1])
        e_ta = ph.do("tensor", tpa, after=after_ta, mark=True)
        ev_avtev[(hh, j)] = ph.do("vector", lambda v, j=j: v.tensor_copy(
            avfmT[:, j * 128:(j + 1) * 128], r.PST[:, 0:128]),
            after=[e_ta], mark=True)
        if j == TT - 1:
            ev_avdma[hh] = ph.do("scalar", lambda s, hh=hh: s.dma_start(
                out=dram["dAV"][hh * 128:(hh + 1) * 128, :], in_=avfmT[:, :]),
                after=[ev_avtev[(hh, j)].h()], mark=True, dma="dSA")

    NOFF = 1
    prev = None
    for step in range(16 + NOFF):
        hh = step
        gi = step - NOFF
        if hh < 16:
            slh = hh % 2

            def ldkv(s, hh=hh, sl=slh):
                return [s.dma_start(out=kh[sl][:, 0:1024],
                                    in_=dram["gK"][hh * 128:(hh + 1) * 128, :]),
                        s.dma_start(out=kh[sl][:, 1024:2048],
                                    in_=dram["gK"][2048 + hh * 128:2048 + (hh + 1) * 128, :]),
                        s.dma_start(out=vh[sl][:, :].rearrange("p (c m) -> p c m", c=16),
                                    in_=dram["gV"][:, hh * 128:(hh + 1) * 128].rearrange(
                                        "(c p) m -> p c m", p=128))]
            after_kv = [cc_kv] + (list(after0) if hh < 2 else
                                 [ev_sc[(hh - 2, TT - 1)], ev_avmm[(hh - 2, TT - 1)]])
            ev_kvld[hh] = ph.do("scalar", ldkv, after=after_kv, mark=True,
                                dma="dS" + str(slh), n=3)
        if gi >= 0:
            slg = gi % 2
            after_g = ([cc_q2, cc_kv] + list(after0) if gi < 2
                       else [ev_gmm[(gi - 2, TT - 1)]])
            ev_gw[gi] = ph.do("gpsimd", lambda g, gi=gi, sl=slg: gu_strip_load(g, gi, sl),
                              after=after_g, mark=True, dma="dW" + str(slg), n=2)
        for t in range(TT):
            if gi >= 0:
                emit_gu_mm(gi, t, slg)
                emit_gu_evac(gi, t, slg)
            if prev is not None:
                emit_attn_b(prev[0], prev[1], prev[0] % 2)
                emit_attn_c(prev[0], prev[1])
            if hh < 16:
                emit_attn_a(hh, t, slh)
                emit_attn_rec(hh, t)
            if gi >= 0:
                ev_sq_last = emit_gu_sq(gi, t, slg)
            prev = (hh, t) if hh < 16 else None
        if gi >= 0:
            ev_hdma[gi] = ph.do("sync", lambda s, gi=gi, sl=slg: s.dma_start(
                out=dram["dH"][:, gi * 512:(gi + 1) * 512].rearrange("(tc p) m -> p tc m", p=128),
                in_=hstage[sl][:, :].rearrange("p (tc m) -> p tc m", tc=TT)),
                after=[ev_hev[(gi, TT - 1)]], mark=True, dma="dL" + str(slg))
    ph.flush()
    return (ev_avdma[15], ev_hdma[15], ev_gmm[(15, TT - 1)], ev_sc[(15, TT - 1)],
            ev_sq_last)


def emit_hn_l1(p, r, dram, gu_done, after0):
    ph = Phase(p)
    ST = r.STATS
    hsl = [r.A2[:, 0:8192], r.A2[:, 8192:16384]]
    hnt = r.A2[:, 16384:24576]
    hfm = r.A2[:, 24576:32768]
    e_parts = []
    for t in range(TT):
        e1 = ph.do("vector", lambda v, t=t: v.reduce_sum(
            ST[:, 11:12], r.SSQ[:, t * 16:(t + 1) * 16], axis=AX.X),
            after=[gu_done] + list(after0), mark=True)
        e2 = ph.do("vector", lambda v: v.tensor_scalar(
            ST[:, 12:13], ST[:, 11:12], 1.0 / 8192.0, None, ALU.mult),
            after=[e1.h()], mark=True)
        e3 = ph.do("scalar", lambda a: a.activation(
            ST[:, 13:14], ST[:, 12:13], AF.Sqrt, bias=r.CONSTS[:, 1:2]),
            after=[e2], mark=True)
        e4 = ph.do("vector", lambda v, t=t: v.reciprocal(r.RST[:, t:t + 1], ST[:, 13:14]),
                   after=[e3], mark=True)
        e_parts.append(e4)
    ev_ld = [None] * TT
    ev_hn = [None] * TT
    ev_tp = {}
    ev_fev = {}
    ev_st = [None] * TT
    for t in range(TT):
        sl = t % 2
        after_l = list(after0)
        if t >= 2:
            after_l.append(ev_hn[t - 2])
        ev_ld[t] = ph.do("scalar", lambda s, t=t, sl=sl: s.dma_start(
            out=hsl[sl][:, :], in_=dram["dH"][t * 128:(t + 1) * 128, :]),
            after=after_l, mark=True, dma="dS" + str(sl))
        after_n = [ev_ld[t], e_parts[t]]
        if t >= 1:
            after_n.append(ev_tp[(t - 1, 7)])
        ev_hn[t] = ph.do("vector", lambda v, t=t, sl=sl: v.tensor_scalar(
            hnt[:, :], hsl[sl][:, :], r.RST[:, t:t + 1], None, ALU.mult),
            after=after_n, mark=True)
        for rd in range(8):
            after_t = [ev_hn[t]]
            if rd >= 1:
                after_t.append(ev_fev[(t, rd - 1)])
            elif t >= 1:
                after_t.append(ev_fev[(t - 1, 7)])
            else:
                after_t += list(after0)

            def tp8(te, rd=rd):
                lastm = None
                for i8 in range(8):
                    c = rd * 8 + i8
                    lastm = te.transpose(r.PST[:, i8 * 128:(i8 + 1) * 128],
                                         hnt[:, c * 128:(c + 1) * 128], r.IDENT[:, :])
                return lastm
            ev_tp[(t, rd)] = ph.do("tensor", tp8, after=after_t, mark=True)
            after_f = [ev_tp[(t, rd)]]
            if rd == 0 and t >= 1:
                after_f.append(ev_st[t - 1])
            ev_fev[(t, rd)] = ph.do("vector", lambda v, rd=rd: v.tensor_copy(
                hfm[:, rd * 1024:(rd + 1) * 1024], r.PST[:, :]),
                after=after_f, mark=True)
        after_s = [ev_fev[(t, 7)]]
        if t >= 2:
            after_s.append(ev_st[t - 2])
        ev_st[t] = ph.do("sync", lambda s, t=t: s.dma_start(
            out=dram["dHN"][:, t * 128:(t + 1) * 128].rearrange("(c p) m -> p c m", p=128),
            in_=hfm[:, :].rearrange("p (c m) -> p c m", c=64)),
            after=after_s, mark=True, dma="dL" + str(t % 2))
    ph.flush()
    return ev_st[TT - 1]


def emit_final_l1(p, r, io, dram, x1_dram, out_dram, hn_ev, av_ev, after0):
    ph = Phase(p)
    hnsl = [r.A2[:, k * 8192:(k + 1) * 8192] for k in range(4)]
    wpsl = [r.QF[:, 0:8192], r.QF[:, 8192:16384]]
    wosl = [r.A1[:, 0:8192], r.A1[:, 8192:16384]]
    avsl = [r.A1[:, 16384:18432], r.A1[:, 18432:20480]]
    x1sl2 = [a_f32(r.XN, 0, 2048), a_f32(r.XN, 12288, 2048)]
    x2sl = [a_f32(r.XN, 4096, 2048), a_f32(r.XN, 8192, 2048)]
    psums = [r.PSA[:, k * 512:(k + 1) * 512] for k in range(4)]
    ev_hn_ld = {}
    ev_wo = {}
    ev_wp = {}
    ev_av = {}
    ev_x1 = {}
    ev_mmq = {}
    ev_mmwo = {}
    ev_mmwo_lin = {}
    ev_ev = {}
    ev_out = {}
    si = 0
    ai = 0
    for hg in range(2):
        after_h = [hn_ev] + (list(after0) if hg == 0 else
                             [ev_mmwo[(0, 3, k)] for k in range(4)])

        def ldhn(s, hg=hg):
            return [s.dma_start(
                out=hnsl[k][:, :].rearrange("p (c m) -> p c m", c=64),
                in_=dram["dHN"][:, (hg * 4 + k) * 128:(hg * 4 + k + 1) * 128].rearrange(
                    "(c p) m -> p c m", p=128)) for k in range(4)]
        ev_hn_ld[hg] = ph.do("scalar", ldhn, after=after_h, mark=True, dma="dS2", n=4)
        for dout in range(4):
            gidx = hg * 4 + dout
            after_wo = list(after0) if gidx < 2 else [ev_mmwo[((gidx - 2) // 4, (gidx - 2) % 4, 3)]]
            ev_wo[gidx] = ph.do("gpsimd", lambda g, dout=dout, sl=gidx % 2: g.dma_start(
                out=wosl[sl][:, :],
                in_=io["woT"][dout * 128:(dout + 1) * 128, :]),
                after=after_wo, mark=True, dma="dW0")
            after_x = list(after0) if gidx < 2 else [ev_ev[(gidx - 2, k)] for k in range(4)]
            ev_x1[gidx] = ph.do("sync", lambda s, dout=dout, hg=hg, sl=gidx % 2: s.dma_start(
                out=x1sl2[sl][:, :].rearrange("p (tc m) -> p tc m", tc=4),
                in_=x1_dram[hg * 512:(hg + 1) * 512, dout * 512:(dout + 1) * 512].rearrange(
                    "(tc p) m -> p tc m", p=128)),
                after=after_x, mark=True, dma="dL" + str(gidx % 2))
            for q in range(8):
                after_wp = list(after0) if si < 2 else [ev_mmq[si - 2]]
                ev_wp[si] = ph.do("gpsimd", lambda g, dout=dout, q=q, sl=si % 2: [
                    g.dma_start(
                        out=wpsl[sl][:, h2 * 2048:(h2 + 1) * 2048],
                        in_=dram["qpO"][q * 1024 + (dout * 2 + h2) * 128:
                                        q * 1024 + (dout * 2 + h2 + 1) * 128, :])
                    for h2 in range(2)],
                    after=after_wp, mark=True, dma="dW1", n=2)

                def mmq(te, q=q, sl=si % 2):
                    lastm = None
                    for k in range(4):
                        for kc in range(8):
                            lastm = te.matmul(psums[k],
                                              hnsl[k][:, (q * 8 + kc) * 128:(q * 8 + kc + 1) * 128],
                                              wpsl[sl][:, (kc // 4) * 2048 + (kc % 4) * 512:
                                                       (kc // 4) * 2048 + (kc % 4) * 512 + 512],
                                              start=(q == 0 and kc == 0), stop=False,
                                              skip_group_check=True)
                    return lastm
                after_m = [ev_wp[si], ev_hn_ld[hg]]
                if q == 0:
                    if gidx == 0:
                        after_m += list(after0)
                    else:
                        after_m += [ev_ev[(gidx - 1, k)] for k in range(4)]
                ev_mmq[si] = ph.do("tensor", mmq, after=after_m, mark=True)
                si += 1
            for k in range(4):
                after_av = [av_ev] + (list(after0) if ai < 2 else [ev_mmwo_lin[ai - 2]])                     if True else []
                ev_av[ai] = ph.do("scalar", lambda s, hg=hg, k=k, sl=ai % 2: s.dma_start(
                    out=avsl[sl][:, :].rearrange("p (c m) -> p c m", c=16),
                    in_=dram["dAV"][:, (hg * 4 + k) * 128:(hg * 4 + k + 1) * 128].rearrange(
                        "(c p) m -> p c m", p=128)),
                    after=after_av, mark=True, dma="dS3")

                def mmwo(te, k=k, sl2=gidx % 2, sla=ai % 2):
                    lastm = None
                    for c in range(16):
                        lastm = te.matmul(psums[k], avsl[sla][:, c * 128:(c + 1) * 128],
                                          wosl[sl2][:, c * 512:(c + 1) * 512],
                                          start=False, stop=(c == 15),
                                          skip_group_check=True)
                    return lastm
                ev_mmwo[(hg, dout, k)] = ph.do("tensor", mmwo,
                                               after=[ev_av[ai], ev_wo[gidx], ev_mmq[si - 1]],
                                               mark=True)
                ev_mmwo_lin[ai] = ev_mmwo[(hg, dout, k)]
                after_e = [ev_mmwo[(hg, dout, k)], ev_x1[gidx]]
                if k == 0 and gidx >= 2:
                    after_e.append(ev_out[gidx - 2])
                ev_ev[(gidx, k)] = ph.do("vector", lambda v, gidx=gidx, k=k: v.tensor_tensor(
                    x2sl[gidx % 2][:, k * 512:(k + 1) * 512], psums[k],
                    x1sl2[gidx % 2][:, k * 512:(k + 1) * 512], ALU.add),
                    after=after_e, mark=True)
                ai += 1
            after_o = [ev_ev[(gidx, k)] for k in range(4)]
            if gidx >= 2:
                after_o.append(ev_out[gidx - 2])
            ev_out[gidx] = ph.do("sync", lambda s, dout=dout, hg=hg, gidx=gidx: s.dma_start(
                out=out_dram[hg * 512:(hg + 1) * 512, dout * 512:(dout + 1) * 512].rearrange(
                    "(tc p) m -> p tc m", p=128),
                in_=x2sl[gidx % 2][:, :].rearrange("p (tc m) -> p tc m", tc=4)),
                after=after_o, mark=True, dma="dL" + str(2 + gidx % 2))
    ph.flush()
    return ev_out[7]


def declare_io_l1(p, with_in=True, with_out=True, with_sel=True):
    nc = p.nc
    io = {}
    def inp(name, shape):
        io[name] = nc.dram_tensor(name, shape, F32, kind="ExternalInput").ap()
    if with_in:
        inp("x1", [T, D])
    inp("wqT", [D, D]); inp("wkT", [D, D])
    inp("wvT", [512, 8192]); inp("woT", [512, 8192])
    inp("wguSL", [D, D]); inp("wpSL", [T, D])
    if with_sel:
        inp("selc", [128, 2]); inp("selr", [2, 128])
    if with_out:
        io["x2o"] = nc.dram_tensor("x2o", [T, D], F32, kind="ExternalOutput").ap()
    return io


def declare_dram_l1(p):
    nc = p.nc
    d = {}
    d["dKin"] = nc.dram_tensor("dKin1", [D, T], BF16).ap()
    d["dVin"] = nc.dram_tensor("dVin1", [T, D], BF16).ap()
    d["gK"] = nc.dram_tensor("gK1", [2 * D, T], BF16).ap()
    d["gV"] = nc.dram_tensor("gV1", [S, D], BF16).ap()
    d["dAV"] = nc.dram_tensor("dAV1", [D, T], BF16).ap()
    d["dH"] = nc.dram_tensor("dH1", [T, 8 * T], BF16).ap()
    d["dHN"] = nc.dram_tensor("dHN1", [8 * T, T], BF16).ap()
    d["qguI"] = nc.dram_tensor("qguI", [D, D], BF16).ap()
    d["qguO"] = nc.dram_tensor("qguO", [8 * D, D], BF16, addr_space="Shared").ap()
    d["qpI"] = nc.dram_tensor("qpI", [T, D], BF16).ap()
    d["qpO"] = nc.dram_tensor("qpO", [8 * T, D], BF16, addr_space="Shared").ap()
    return d


def emit_l1_body(p, r, io, dram, x_in_ap, out_ap, ev0):
    qst = {}
    gu_tiles = [(tc, tc) for tc in range(16)]
    p_tiles = [(tc, tc) for tc in range(8)]
    ev_xn = emit_ln(p, r, x_in_ap, list(ev0), sq_scratch=a_f32(r.A1, 10240, 2048))
    emit_quant_tiles(p, r, io, "wguSL", dram["qguI"], gu_tiles[0:6], qst, list(ev0))
    ev_k, _ = emit_k_l1(p, r, io, dram, ev_xn, list(ev0), dst_kind="dram")
    emit_quant_tiles(p, r, io, "wguSL", dram["qguI"], gu_tiles[6:12], qst, list(ev0))
    ev_v, _ = emit_v_l0(p, r, io, "wvT", dram["dVin"], ev_xn, [ev_k, ev_xn],
                        use_psa=False, vst_off=(4096, 6144))
    ev_qg = emit_quant_tiles(p, r, io, "wguSL", dram["qguI"], gu_tiles[12:16], qst, list(ev0))
    ph = Phase(p)
    cc_k = ph.do("gpsimd", lambda g: g.collective_compute(
        "AllGather", ALU.bypass, replica_groups=PAIRS,
        ins=[dram["dKin"]], outs=[dram["gK"]]), after=[ev_v], mark=True, cc=True)
    cc_v = ph.do("gpsimd", lambda g: g.collective_compute(
        "AllGather", ALU.bypass, replica_groups=PAIRS,
        ins=[dram["dVin"]], outs=[dram["gV"]]), mark=True, cc=True)
    ph.flush()
    ev_q, _ = emit_k_l1(p, r, io, dram, ev_xn, [ev_v], scale=SC1, dst_kind="qf")
    ph = Phase(p)
    cc_g = ph.do("gpsimd", lambda g: g.collective_compute(
        "AllGather", ALU.bypass, replica_groups=ALL8,
        ins=[dram["qguI"]], outs=[dram["qguO"]]), after=[ev_qg], mark=True, cc=True)
    ph.flush()
    ev_qp = emit_quant_tiles(p, r, io, "wpSL", dram["qpI"], p_tiles, qst, list(ev0))
    ph = Phase(p)
    cc_p = ph.do("gpsimd", lambda g: g.collective_compute(
        "AllGather", ALU.bypass, replica_groups=ALL8,
        ins=[dram["qpI"]], outs=[dram["qpO"]]), after=[ev_qp], mark=True, cc=True)
    ph.flush()
    ev_avd, ev_hd, ev_gmm_last, ev_sc_last, ev_sq_last = emit_gu_attn_l1(
        p, r, io, dram, ev_xn, ev_q, cc_v, cc_g, [ev_q, ev_v, ev_qp])
    ev_hn = emit_hn_l1(p, r, dram, ev_hd, [ev_gmm_last, ev_sc_last, ev_avd, ev_sq_last])
    ev_x2 = emit_final_l1(p, r, io, dram, x_in_ap, out_ap, ev_hn, ev_avd,
                          [ev_hn, ev_avd, cc_p])
    return ev_x2


def build_l1_program():
    p = Prog()
    r = alloc_resources(p)
    io = declare_io_l1(p)
    dram = declare_dram_l1(p)
    ev_c, ev_s = emit_consts(p, r, io)
    emit_l1_body(p, r, io, dram, io["x1"], io["x2o"], [ev_c, ev_s])
    emit_tail_wait(p)
    return p, io


def prep_shared_l1(inputs):
    sh = {}
    sh["wqT"] = stripify(np.asarray(inputs["L1_attn_w_q"], np.float32).T, 128)
    sh["wkT"] = stripify(np.asarray(inputs["L1_attn_w_k"], np.float32).T, 128)
    sh["wvT"] = stripify(np.asarray(inputs["L1_attn_w_v"], np.float32).T, 512)
    sh["woT"] = stripify(np.asarray(inputs["L1_attn_w_o"], np.float32).T, 512)
    selc = np.zeros((128, 2), np.float32)
    selc[0:64, 0] = 1.0
    selc[64:128, 1] = 1.0
    sh["selc"] = selc
    sh["selr"] = np.ascontiguousarray(selc.T)
    return sh


def make_in_maps_l1(inputs, x1_full):
    sh = prep_shared_l1(inputs)
    wguT = np.ascontiguousarray(np.asarray(inputs["L1_ffn_w_gate_up"], np.float32).T)
    wpT = np.ascontiguousarray(np.asarray(inputs["L1_ffn_w_proj"], np.float32).T)
    maps = []
    for c in range(8):
        b, h = c // 2, c % 2
        m = dict(sh)
        m["x1"] = np.ascontiguousarray(x1_full[b, h * 1024:(h + 1) * 1024, :])
        m["wguSL"] = np.ascontiguousarray(wguT[:, c * 2048:(c + 1) * 2048])
        m["wpSL"] = _perm_wp(wpT[c * 1024:(c + 1) * 1024, :])
        maps.append(m)
    return maps


def _perm_wp(blk):
    B = blk.reshape(2, 4, 128, 4, 512).transpose(3, 0, 2, 1, 4).reshape(1024, 2048)
    return np.ascontiguousarray(B)


_L1_CACHE = {}


def run_l1(inputs, x1_full):
    if "p" not in _L1_CACHE:
        p, io = build_l1_program()
        _L1_CACHE["p"] = p
    p = _L1_CACHE["p"]
    maps = make_in_maps_l1(inputs, x1_full)
    res = run_bass_kernel_spmd(p.nc, maps, core_ids=list(range(8)))
    out = np.zeros((4, S, D), np.float32)
    for c in range(8):
        b, h = c // 2, c % 2
        out[b, h * 1024:(h + 1) * 1024, :] = res.results[c]["x2o"]
    return out



def build_fused_program():
    p = Prog()
    r = alloc_resources(p)
    io = {}
    io.update(declare_io_l0(p, with_out=False))
    io.update(declare_io_l1(p, with_in=False, with_out=True, with_sel=False))
    dram0 = declare_dram_shared(p)
    dram1 = declare_dram_l1(p)
    dX1 = p.nc.dram_tensor("dX1", [T, D], F32).ap()
    ev_c, ev_s = emit_consts(p, r, io)
    ev_x1 = emit_l0_body(p, r, io, dram0, io["x0"], dX1, [ev_c, ev_s])
    emit_l1_body(p, r, io, dram1, dX1, io["x2o"], [ev_x1])
    emit_tail_wait(p)
    return p, io


_F_CACHE = {}


def make_in_maps_fused(inputs):
    m0 = make_in_maps_l0(inputs)
    sh1 = prep_shared_l1(inputs)
    wguT = np.ascontiguousarray(np.asarray(inputs["L1_ffn_w_gate_up"], np.float32).T)
    wpT = np.ascontiguousarray(np.asarray(inputs["L1_ffn_w_proj"], np.float32).T)
    maps = []
    for c in range(8):
        m = dict(m0[c])
        for k, v in sh1.items():
            m[k] = v
        m["wguSL"] = np.ascontiguousarray(wguT[:, c * 2048:(c + 1) * 2048])
        m["wpSL"] = _perm_wp(wpT[c * 1024:(c + 1) * 1024, :])
        maps.append(m)
    return maps


def kernel(**inputs):
    if "p" not in _F_CACHE:
        _F_CACHE["p"] = build_fused_program()[0]
    p = _F_CACHE["p"]
    maps = make_in_maps_fused(inputs)
    res = run_bass_kernel_spmd(p.nc, maps, core_ids=list(range(8)))
    out = np.zeros((4, S, D), np.float32)
    for c in range(8):
        b, h = c // 2, c % 2
        out[b, h * 1024:(h + 1) * 1024, :] = res.results[c]["x2o"]
    return out
